# version 01
# baseline (speedup 1.0000x reference)
import contextlib
import numpy as np
import ml_dtypes
import concourse.bass as bass
import concourse.mybir as mybir
from concourse.bass_utils import run_bass_kernel_spmd

F32 = mybir.dt.float32
BF16 = mybir.dt.bfloat16
I32 = mybir.dt.int32
AF = mybir.ActivationFunctionType
ALU = mybir.AluOpType
AX = mybir.AxisListType

NCORES = 8
D = 2048
T_LOC = 2048
T_FULL = 4096
DC = D // 128

bf16 = ml_dtypes.bfloat16


class Prog:
    ENGINES = ("tensor", "vector", "scalar", "gpsimd", "sync")

    def __init__(self, nc, stack):
        self.nc = nc
        self.sems = {}
        self.cnt = {}
        self.dsems = {}
        self.dcnt = {}
        self.thunks = {e: [] for e in self.ENGINES}
        self.last_ev = {e: None for e in self.ENGINES}
        for e in self.ENGINES:
            self.sems[e] = stack.enter_context(nc.semaphore(f"sem_{e}"))
            self.dsems[e] = stack.enter_context(nc.semaphore(f"dsem_{e}"))
            self.cnt[e] = 0
            self.dcnt[e] = 0

    def op(self, eng, fn, deps=(), inc=1, chain=True, dma=False):
        my_sem = self.dsems[eng] if dma else self.sems[eng]
        seen = {}
        auto = []
        if chain and eng != "tensor" and self.last_ev[eng] is not None:
            auto = [self.last_ev[eng]]
        for ev in list(deps) + auto:
            if ev is None:
                continue
            sem, val = ev
            key = id(sem)
            if key not in seen or seen[key][1] < val:
                seen[key] = (sem, val)
        dep_list = list(seen.values())
        if dma:
            self.dcnt[eng] += inc
            n_after = self.dcnt[eng]
        else:
            self.cnt[eng] += inc
            n_after = self.cnt[eng]

        def thunk(e, fn=fn, dep_list=dep_list, inc=inc, my_sem=my_sem):
            for sem, val in dep_list:
                e.wait_ge(sem, val)
            inst = fn(e)
            if inc:
                if isinstance(inst, (list, tuple)):
                    inst = inst[-1]
                inst.then_inc(my_sem, inc)

        self.thunks[eng].append(thunk)
        ev = (my_sem, n_after)
        self.last_ev[eng] = ev
        return ev

    def dma(self, eng, out, in_, deps=(), **kw):
        return self.op(eng, lambda e: e.dma_start(out=out, in_=in_, **kw),
                       deps=deps, inc=16, dma=True)

    def final_barrier(self):
        for e in self.ENGINES:
            if not self.thunks[e]:
                continue
            me = self

            def thunk(eng, e=e):
                if me.cnt[e]:
                    eng.wait_ge(me.sems[e], me.cnt[e])
                if me.dcnt[e]:
                    eng.wait_ge(me.dsems[e], me.dcnt[e])

            self.thunks[e].append(thunk)

    def finalize(self, block):
        for e in self.ENGINES:
            if not self.thunks[e]:
                continue
            thunks = self.thunks[e]

            def body(engine, thunks=thunks):
                for t in thunks:
                    t(engine)

            getattr(block, e)(body)


def build_and_finalize(nc, emit, *args, **kwargs):
    with contextlib.ExitStack() as st:
        P = Prog(nc, st)
        emit(nc, P, st, *args, **kwargs)
        P.final_barrier()
        with nc.Block() as block:
            P.finalize(block)
    return nc


def emit_A(nc, P, st, stages=99, ntc=None):
    x = nc.dram_tensor("x", [D, T_LOC], F32, kind="ExternalInput").ap()
    xpw = nc.dram_tensor("xpw", [D, 160], F32, kind="ExternalInput").ap()
    dtw = nc.dram_tensor("dtw", [128, D], F32, kind="ExternalInput").ap()
    dtb = nc.dram_tensor("dtb", [128, DC], F32, kind="ExternalInput").ap()
    onesf = nc.dram_tensor("onesf", [1, 128], F32, kind="ExternalInput").ap()
    xn_o = nc.dram_tensor("xn", [D, T_LOC], BF16, kind="ExternalOutput").ap()
    xdbl_o = nc.dram_tensor("xdbl", [160, T_LOC], F32, kind="ExternalOutput").ap()
    delta_o = nc.dram_tensor("delta", [D, T_LOC], F32, kind="ExternalOutput").ap()

    TC = 512
    NTC = ntc if ntc is not None else T_LOC // TC
    x_t = x.rearrange("(c p) t -> p c t", p=128)
    xn_t = xn_o.rearrange("(c p) t -> p c t", p=128)
    delta_t = delta_o.rearrange("(c p) t -> p c t", p=128)

    sb = lambda name, shape, dt_: st.enter_context(nc.sbuf_tensor(name, shape, dt_))
    ps = lambda name, shape, dt_=F32: st.enter_context(nc.psum_tensor(name, shape, dt_))

    xbuf = sb("xbuf", [128, DC * TC], F32)
    xbf = sb("xbf", [128, DC * TC], BF16)
    sqbf = sb("sqbf", [128, DC * TC], BF16)
    xnbf = sb("xnbf", [128, DC * TC], BF16)
    dtile = sb("dtile", [128, DC * TC], F32)
    ones_b = sb("ones_b", [128, 1], BF16)
    ones_f = sb("ones_f", [1, 128], F32)
    xpw_sb = sb("xpw_sb", [128, DC * 160], BF16)
    dtw_sb = sb("dtw_sb", [128, D], BF16)
    dtb_sb = sb("dtb_sb", [128, DC], F32)
    rows = sb("rows", [1, 5 * TC], F32)
    dt16 = sb("dt16", [128, TC], BF16)
    zf = sb("zf", [128, TC], F32)
    az = sb("az", [128, TC], F32)
    el = sb("el", [128, TC], F32)
    rz = sb("rz", [128, TC], F32)
    xdbl_f = sb("xdbl_f", [128, TC], F32)
    bc_f = sb("bc_f", [32, TC], F32)
    cst = sb("cst", [128, 2], F32)

    p_mean = ps("p_mean", [1, TC])
    p_msq = ps("p_msq", [1, TC])
    p_a = ps("p_a", [128, TC])
    p_b = ps("p_b", [128, TC])
    p_dt = ps("p_dt", [128, TC])
    p_bc = ps("p_bc", [32, TC])
    p_z = ps("p_z", [128, TC])

    ev_w1 = P.dma("gpsimd", xpw_sb[:, :].rearrange("p (c e) -> p c e", e=160),
                  xpw.rearrange("(c p) e -> p c e", p=128))
    ev_w2 = P.dma("gpsimd", dtw_sb[:, :], dtw[:, :])
    ev_w3 = P.dma("gpsimd", dtb_sb[:, :], dtb[:, :])
    ev_of = P.dma("gpsimd", ones_f[:, :], onesf[:, :])
    ev_ob = P.op("gpsimd", lambda e: e.memset(ones_b[:, :], 1.0))
    P.op("gpsimd", lambda e: e.memset(cst[:, 0:1], 1e-5))
    ev_cst = P.op("gpsimd", lambda e: e.memset(cst[:, 1:2], 1.0))

    ev_sq = ev_xc = None
    ev_apply_last = None
    ev_xn_store_prev = ev_mm_bc_prev = None
    ev_e2 = ev_rz = ev_dl = None
    ev_xdbl_store = ev_bc_store = None
    ev_dstore = None
    ev_evdt_done = None
    ev_dt16_free = None
    ev_z_read = None
    prev_stats_done = None
    prev_bcast_done = None

    for tc in range(NTC):
        if stages < 1:
            break
        tsl = slice(tc * TC, (tc + 1) * TC)
        ev_x = P.dma("gpsimd", xbuf[:, :], x_t[:, :, tsl],
                     deps=[ev_sq, ev_xc, ev_apply_last])
        ev_xc = P.op("vector", lambda e: e.tensor_copy(xbf[:, :], xbuf[:, :]), deps=[ev_x])
        ev_sq = P.op("scalar", lambda e: e.activation(sqbf[:, :], xbuf[:, :], AF.Square), deps=[ev_x])

        if stages < 2:
            continue
        for c in range(DC):
            last = c == DC - 1
            P.op("tensor", lambda e, c=c, last=last: e.matmul(
                p_mean[:, :], ones_b[:, :], xbf[:, c * TC:(c + 1) * TC],
                start=(c == 0), stop=last),
                deps=[ev_xc, ev_ob, prev_stats_done] if c == 0 else (), inc=1 if last else 0)
        ev_mm_mean = P.last_ev["tensor"]
        for c in range(DC):
            last = c == DC - 1
            P.op("tensor", lambda e, c=c, last=last: e.matmul(
                p_msq[:, :], ones_b[:, :], sqbf[:, c * TC:(c + 1) * TC],
                start=(c == 0), stop=last),
                deps=[ev_sq, prev_stats_done] if c == 0 else (), inc=1 if last else 0)
        ev_mm_msq = P.last_ev["tensor"]

        if stages < 3:
            continue
        R = lambda i: rows[0:1, i * TC:(i + 1) * TC]
        ev_r0 = P.op("scalar", lambda e: e.activation(R(0), p_mean[0:1, :], AF.Copy, scale=1.0 / D), deps=[ev_mm_mean])
        ev_r1 = P.op("scalar", lambda e: e.activation(R(1), p_msq[0:1, :], AF.Copy, scale=1.0 / D), deps=[ev_mm_msq])
        ev_r2 = P.op("scalar", lambda e: e.activation(R(2), R(0), AF.Square), deps=[ev_r0])
        ev_r3 = P.op("vector", lambda e: e.tensor_sub(R(2), R(1), R(2)), deps=[ev_r1, ev_r2])
        ev_r4 = P.op("scalar", lambda e: e.activation(R(3), R(2), AF.Ln, bias=cst[0:1, 0:1]), deps=[ev_r3, ev_cst])
        ev_rstd = P.op("scalar", lambda e: e.activation(R(3), R(3), AF.Exp, scale=-0.5), deps=[ev_r4])
        ev_brow = P.op("vector", lambda e: e.tensor_mul(R(4), R(0), R(3)), deps=[ev_rstd])
        prev_stats_done = ev_r1

        if stages < 4:
            continue
        bd = [ev_rstd, ev_brow, ev_of, prev_bcast_done]
        ev_ba = P.op("tensor", lambda e: e.matmul(p_a[:, :], ones_f[:, :], R(3), start=True, stop=True), deps=bd)
        ev_bb = P.op("tensor", lambda e: e.matmul(p_b[:, :], ones_f[:, :], R(4), start=True, stop=True), deps=bd)

        dep0 = [ev_ba, ev_bb, ev_x, ev_dstore, ev_xn_store_prev, ev_mm_bc_prev]
        for c in range(DC):
            csl = slice(c * TC, (c + 1) * TC)
            P.op("vector", lambda e, csl=csl: e.tensor_mul(dtile[:, csl], xbuf[:, csl], p_a[:, :]),
                 deps=dep0 if c == 0 else ())
            P.op("vector", lambda e, csl=csl: e.tensor_sub(xnbf[:, csl], dtile[:, csl], p_b[:, :]))
        ev_apply_last = P.last_ev["vector"]
        prev_bcast_done = ev_apply_last
        ev_xn_store = P.dma("gpsimd", xn_t[:, :, tsl], xnbf[:, :], deps=[ev_apply_last])

        if stages < 5:
            continue
        for c in range(DC):
            last = c == DC - 1
            P.op("tensor", lambda e, c=c, last=last: e.matmul(
                p_dt[:, :], xpw_sb[:, c * 160:c * 160 + 128], xnbf[:, c * TC:(c + 1) * TC],
                start=(c == 0), stop=last),
                deps=[ev_apply_last, ev_w1, ev_evdt_done, ev_e2] if c == 0 else (), inc=1 if last else 0)
        ev_mm_dt = P.last_ev["tensor"]
        if stages >= 5.2:
            for c in range(DC):
                last = c == DC - 1
                P.op("tensor", lambda e, c=c, last=last: e.matmul(
                    p_bc[:, :], xpw_sb[:, c * 160 + 128:(c + 1) * 160], xnbf[:, c * TC:(c + 1) * TC],
                    start=(c == 0), stop=last), deps=(), inc=1 if last else 0)
            ev_mm_bc = P.last_ev["tensor"]

        ev_e1 = P.op("vector", lambda e: e.tensor_copy(xdbl_f[:, :], p_dt[:, :]), deps=[ev_mm_dt, ev_xdbl_store])
        if stages >= 5.3:
            ev_e2 = P.op("vector", lambda e: e.tensor_copy(dt16[:, :], p_dt[:, :]),
                         deps=[ev_mm_dt, ev_dt16_free])
        ev_evdt_done = ev_e1
        ev_xdbl_store = P.dma("gpsimd", xdbl_o[0:128, tsl], xdbl_f[:, :], deps=[ev_e1])
        if stages >= 5.2:
            ev_e3 = P.op("vector", lambda e: e.tensor_copy(bc_f[:, :], p_bc[:, :]), deps=[ev_mm_bc, ev_bc_store])
            ev_evdt_done = ev_e3
            ev_bc_store = P.dma("gpsimd", xdbl_o[128:160, tsl], bc_f[:, :], deps=[ev_e3])

        if stages < 6:
            continue
        ev_dl_all = []
        for dci in range(DC):
            ev_mm_z = P.op("tensor", lambda e, dci=dci: e.matmul(
                p_z[:, :], dtw_sb[:, dci * 128:(dci + 1) * 128], dt16[:, :],
                start=True, stop=True), deps=[ev_e2, ev_w2, ev_z_read], inc=1)
            ev_z = P.op("scalar", lambda e, dci=dci: e.activation(
                zf[:, :], p_z[:, :], AF.Identity, bias=dtb_sb[:, dci:dci + 1]),
                deps=[ev_mm_z, ev_w3, ev_rz])
            ev_z_read = ev_z
            ev_az = P.op("scalar", lambda e: e.activation(az[:, :], zf[:, :], AF.Abs), deps=[ev_z])
            ev_el = P.op("scalar", lambda e: e.activation(el[:, :], az[:, :], AF.Exp, scale=-1.0), deps=[ev_az, ev_dl])
            ev_l1 = P.op("scalar", lambda e: e.activation(el[:, :], el[:, :], AF.Ln, bias=cst[:, 1:2]), deps=[ev_el])
            ev_rz = P.op("vector", lambda e: e.tensor_scalar(rz[:, :], zf[:, :], 0.0, None, ALU.max), deps=[ev_z])
            ev_dl = P.op("vector", lambda e, dci=dci: e.tensor_add(
                dtile[:, dci * TC:(dci + 1) * TC], rz[:, :], el[:, :]),
                deps=[ev_rz, ev_l1] + ([ev_apply_last, ev_dstore] if dci == 0 else []))
            ev_dl_all.append(ev_dl)
        ev_dt16_free = P.last_ev["tensor"]
        ev_dstore = P.dma("gpsimd", delta_t[:, :, tsl], dtile[:, :], deps=[ev_dl_all[-1]])
        ev_xn_store_prev = ev_xn_store
        ev_mm_bc_prev = ev_mm_bc


def build_A(stages=99, ntc=None):
    nc = bass.Bass("TRN2", target_bir_lowering=False)
    return build_and_finalize(nc, emit_A, stages=stages, ntc=ntc)


def _fm(a):
    return np.ascontiguousarray(np.asarray(a).T)


def _core_tokens(x, c):
    b, h = c // 2, c % 2
    return x[b, h * T_LOC:(h + 1) * T_LOC, :]


def run_A(inputs):
    x_full = inputs["L0_attn_x"]
    nc = build_A()
    xpw_t = np.ascontiguousarray(inputs["L0_attn_x_proj_w"].T)
    dtw_t = np.ascontiguousarray(inputs["L0_attn_dt_proj_w"].T)
    dtb = np.ascontiguousarray(inputs["L0_attn_dt_proj_b"].reshape(DC, 128).T)
    onesf = np.ones((1, 128), np.float32)
    in_maps = []
    for c in range(NCORES):
        xs = _fm(_core_tokens(x_full, c))
        in_maps.append({"x": xs, "xpw": xpw_t, "dtw": dtw_t, "dtb": dtb, "onesf": onesf})
    res = run_bass_kernel_spmd(nc, in_maps, core_ids=list(range(NCORES)))
    return res.results




DB = 8
TCB = 1024
NTCB = T_FULL // TCB


def emit_B(nc, P, st):
    delta_i = nc.dram_tensor("delta", [1024, T_FULL], F32, kind="ExternalInput").ap()
    xn_i = nc.dram_tensor("xn", [1024, T_FULL], BF16, kind="ExternalInput").ap()
    Bc_i = nc.dram_tensor("Bc", [16, T_FULL], F32, kind="ExternalInput").ap()
    Cc_i = nc.dram_tensor("Cc", [16, T_FULL], F32, kind="ExternalInput").ap()
    Alog_i = nc.dram_tensor("Alog", [128, DB * 16], F32, kind="ExternalInput").ap()
    Dp_i = nc.dram_tensor("Dp", [128, DB], F32, kind="ExternalInput").ap()
    s_o = nc.dram_tensor("s", [1024, T_FULL], F32, kind="ExternalOutput").ap()

    delta_t = delta_i.rearrange("(c p) t -> p c t", p=128)
    xn_t = xn_i.rearrange("(c p) t -> p c t", p=128)
    s_t = s_o.rearrange("(c p) t -> p c t", p=128)

    sb = lambda name, shape, dt_: st.enter_context(nc.sbuf_tensor(name, shape, dt_))

    dlt = sb("dlt", [128, DB * TCB], F32)
    xnt = sb("xnt", [128, DB * TCB], BF16)
    ut = sb("ut", [128, DB * TCB], F32)
    yt = sb("yt", [128, DB * TCB], F32)
    bcB = sb("bcB", [128, TCB], F32)
    bcC = sb("bcC", [128, TCB], F32)
    at = sb("at", [128, TCB], F32)
    bbt = sb("bbt", [128, TCB], F32)
    ht = sb("ht", [128, TCB], F32)
    tmpt = sb("tmpt", [128, TCB], F32)
    A2 = sb("A2", [128, DB * 16], F32)
    Dsb = sb("Dsb", [128, DB], F32)
    carry = sb("carry", [128, DB * 16], F32)

    ev_al = P.dma("gpsimd", A2[:, :], Alog_i[:, :])
    ev_dp = P.dma("gpsimd", Dsb[:, :], Dp_i[:, :])
    ev_a1 = P.op("scalar", lambda e: e.activation(A2[:, :], A2[:, :], AF.Exp), deps=[ev_al])
    ev_a2 = P.op("vector", lambda e: e.tensor_scalar(A2[:, :], A2[:, :], -1.0, None, ALU.mult), deps=[ev_a1])
    ev_cinit = P.op("vector", lambda e: e.memset(carry[:, :], 0.0))

    ev_sstore = None
    ev_a_used = None
    ev_bc_used = None
    for tc in range(NTCB):
        tsl = slice(tc * TCB, (tc + 1) * TCB)
        ev_dl = P.dma("sync", dlt[:, :].rearrange("p (c t) -> p c t", t=TCB),
                      delta_t[:, :, tsl], deps=[P.last_ev["vector"], ev_a_used])
        ev_xl = P.dma("sync", xnt[:, :].rearrange("p (c t) -> p c t", t=TCB),
                      xn_t[:, :, tsl], deps=[P.last_ev["vector"]])
        for dc_ in range(DB):
            csl = slice(dc_ * TCB, (dc_ + 1) * TCB)
            P.op("vector", lambda e, csl=csl: e.tensor_mul(ut[:, csl], dlt[:, csl], xnt[:, csl]),
                 deps=[ev_dl, ev_xl, ev_sstore])
            P.op("vector", lambda e, csl=csl, dc_=dc_: e.tensor_scalar(
                yt[:, csl], xnt[:, csl], Dsb[:, dc_:dc_ + 1], None, ALU.mult), deps=[ev_dp])
        for n in range(16):
            ev_bB = P.dma("gpsimd", bcB[:, :], Bc_i[n:n + 1, tsl].partition_broadcast(128),
                          deps=[ev_bc_used])
            ev_bC = P.dma("gpsimd", bcC[:, :], Cc_i[n:n + 1, tsl].partition_broadcast(128),
                          deps=[ev_bc_used])
            for dc_ in range(DB):
                csl = slice(dc_ * TCB, (dc_ + 1) * TCB)
                idx = dc_ * 16 + n
                ev_a = P.op("scalar", lambda e, csl=csl, idx=idx: e.activation(
                    at[:, :], dlt[:, csl], AF.Exp, scale=A2[:, idx:idx + 1]),
                    deps=[ev_a2, ev_dl, P.last_ev["vector"]])
                P.op("vector", lambda e, csl=csl: e.tensor_mul(bbt[:, :], ut[:, csl], bcB[:, :]),
                     deps=[ev_bB])
                ev_scan = P.op("vector", lambda e, idx=idx: e.tensor_tensor_scan(
                    ht[:, :], at[:, :], bbt[:, :], carry[:, idx:idx + 1], ALU.mult, ALU.add),
                    deps=[ev_a, ev_cinit])
                ev_a_used = ev_scan
                P.op("vector", lambda e, idx=idx: e.tensor_copy(carry[:, idx:idx + 1], ht[:, TCB - 1:TCB]))
                P.op("vector", lambda e: e.tensor_mul(tmpt[:, :], ht[:, :], bcC[:, :]), deps=[ev_bC])
                ev_y = P.op("vector", lambda e, csl=csl: e.tensor_add(yt[:, csl], yt[:, csl], tmpt[:, :]))
            ev_bc_used = ev_y
        for dc_ in range(DB):
            csl = slice(dc_ * TCB, (dc_ + 1) * TCB)
            P.op("vector", lambda e, csl=csl: e.tensor_scalar(yt[:, csl], yt[:, csl], 0.5, None, ALU.mult))
        ev_sstore = P.dma("sync", s_t[:, :, tsl], yt[:, :].rearrange("p (c t) -> p c t", t=TCB),
                          deps=[P.last_ev["vector"]])


def build_B():
    nc = bass.Bass("TRN2", target_bir_lowering=False)
    return build_and_finalize(nc, emit_B)


def prep_B_inputs(inputs, resA):
    A_log = inputs["L0_attn_A_log"]
    Dp = inputs["L0_attn_D_param"]
    maps = []
    for c in range(NCORES):
        b, h = c // 2, c % 2
        cA, cB2 = 2 * b, 2 * b + 1
        rows = slice(1024 * h, 1024 * h + 1024)
        delta_s = np.concatenate([resA[cA]["delta"][rows], resA[cB2]["delta"][rows]], axis=1)
        xn_s = np.concatenate([resA[cA]["xn"][rows], resA[cB2]["xn"][rows]], axis=1)
        Bc = np.concatenate([resA[cA]["xdbl"][128:144], resA[cB2]["xdbl"][128:144]], axis=1)
        Cc = np.concatenate([resA[cA]["xdbl"][144:160], resA[cB2]["xdbl"][144:160]], axis=1)
        Alog_s = np.ascontiguousarray(
            A_log[1024 * h:1024 * h + 1024].reshape(DB, 128, 16).transpose(1, 0, 2).reshape(128, DB * 16))
        Dp_s = np.ascontiguousarray(Dp[1024 * h:1024 * h + 1024].reshape(DB, 128).T)
        maps.append({"delta": np.ascontiguousarray(delta_s),
                     "xn": np.ascontiguousarray(xn_s),
                     "Bc": np.ascontiguousarray(Bc), "Cc": np.ascontiguousarray(Cc),
                     "Alog": Alog_s, "Dp": Dp_s})
    return maps


def run_B(inputs, resA):
    nc = build_B()
    maps = prep_B_inputs(inputs, resA)
    res = run_bass_kernel_spmd(nc, maps, core_ids=list(range(NCORES)))
    return res.results




def emit_ln(P, bufs, x_sb, out_bf, n_dc, TC, deps, prev_stats_done, prev_bcast_done):
    ev_xc = P.op("vector", lambda e: e.tensor_copy(bufs["xc"][:, 0:n_dc * TC], x_sb), deps=deps)
    ev_sq = P.op("scalar", lambda e: e.activation(bufs["sq"][:, 0:n_dc * TC], x_sb, AF.Square), deps=deps)
    p_mean, p_msq, p_a, p_b = bufs["p_mean"], bufs["p_msq"], bufs["p_a"], bufs["p_b"]
    rows = bufs["rows"]
    for c in range(n_dc):
        last = c == n_dc - 1
        P.op("tensor", lambda e, c=c, last=last: e.matmul(
            p_mean[:, :], bufs["ones_b"][:, :], bufs["xc"][:, c * TC:(c + 1) * TC],
            start=(c == 0), stop=last),
            deps=[ev_xc, prev_stats_done] if c == 0 else (), inc=1 if last else 0)
    ev_mm_mean = P.last_ev["tensor"]
    for c in range(n_dc):
        last = c == n_dc - 1
        P.op("tensor", lambda e, c=c, last=last: e.matmul(
            p_msq[:, :], bufs["ones_b"][:, :], bufs["sq"][:, c * TC:(c + 1) * TC],
            start=(c == 0), stop=last),
            deps=[ev_sq, prev_stats_done] if c == 0 else (), inc=1 if last else 0)
    ev_mm_msq = P.last_ev["tensor"]
    R = lambda i: rows[0:1, i * TC:(i + 1) * TC]
    sc = 1.0 / (n_dc * 128)
    ev_r0 = P.op("scalar", lambda e: e.activation(R(0), p_mean[0:1, :], AF.Copy, scale=sc), deps=[ev_mm_mean])
    ev_r1 = P.op("scalar", lambda e: e.activation(R(1), p_msq[0:1, :], AF.Copy, scale=sc), deps=[ev_mm_msq])
    ev_r2 = P.op("scalar", lambda e: e.activation(R(2), R(0), AF.Square), deps=[ev_r0])
    ev_r3 = P.op("vector", lambda e: e.tensor_sub(R(2), R(1), R(2)), deps=[ev_r1, ev_r2])
    ev_r4 = P.op("scalar", lambda e: e.activation(R(3), R(2), AF.Ln, bias=bufs["cst"][0:1, 0:1]), deps=[ev_r3])
    ev_rstd = P.op("scalar", lambda e: e.activation(R(3), R(3), AF.Exp, scale=-0.5), deps=[ev_r4])
    ev_brow = P.op("vector", lambda e: e.tensor_mul(R(4), R(0), R(3)), deps=[ev_rstd])
    bd = [ev_rstd, ev_brow, prev_bcast_done]
    ev_ba = P.op("tensor", lambda e: e.matmul(p_a[:, :], bufs["ones_f"][:, :], R(3), start=True, stop=True), deps=bd)
    ev_bb = P.op("tensor", lambda e: e.matmul(p_b[:, :], bufs["ones_f"][:, :], R(4), start=True, stop=True), deps=bd)
    dep0 = [ev_ba, ev_bb]
    for c in range(n_dc):
        csl = slice(c * TC, (c + 1) * TC)
        P.op("vector", lambda e, csl=csl: e.tensor_mul(bufs["t1"][:, 0:TC], x_sb[:, csl], p_a[:, :]),
             deps=dep0 if c == 0 else ())
        P.op("vector", lambda e, csl=csl: e.tensor_sub(out_bf[:, csl], bufs["t1"][:, 0:TC], p_b[:, :]))
    ev_done = P.last_ev["vector"]
    return ev_done, ev_r1, ev_done


IC0 = 5632 // 128


def emit_C(nc, P, st):
    xn_i = nc.dram_tensor("xn", [D, T_LOC], BF16, kind="ExternalInput").ap()
    x_i = nc.dram_tensor("x", [D, T_LOC], F32, kind="ExternalInput").ap()
    s_i = nc.dram_tensor("s", [D, T_LOC], F32, kind="ExternalInput").ap()
    gwt = nc.dram_tensor("gwt", [D, 5632], F32, kind="ExternalInput").ap()
    uwt = nc.dram_tensor("uwt", [D, 5632], F32, kind="ExternalInput").ap()
    dwt = nc.dram_tensor("dwt", [5632, D], F32, kind="ExternalInput").ap()
    gs_i = nc.dram_tensor("gs", [128, IC0], F32, kind="ExternalInput").ap()
    us_i = nc.dram_tensor("us", [128, IC0], F32, kind="ExternalInput").ap()
    ds_i = nc.dram_tensor("ds", [128, DC], F32, kind="ExternalInput").ap()
    onesf = nc.dram_tensor("onesf", [1, 128], F32, kind="ExternalInput").ap()
    x1_o = nc.dram_tensor("x1", [D, T_LOC], F32, kind="ExternalOutput").ap()
    xn1_o = nc.dram_tensor("xn1", [D, T_LOC], BF16, kind="ExternalOutput").ap()

    TC = 512
    NTC = T_LOC // TC
    xn_t = xn_i.rearrange("(c p) t -> p c t", p=128)
    x_t = x_i.rearrange("(c p) t -> p c t", p=128)
    s_t = s_i.rearrange("(c p) t -> p c t", p=128)
    x1_t = x1_o.rearrange("(c p) t -> p c t", p=128)
    xn1_t = xn1_o.rearrange("(c p) t -> p c t", p=128)
    gwt_t = gwt.rearrange("(c p) i -> p c i", p=128)
    uwt_t = uwt.rearrange("(c p) i -> p c i", p=128)
    dwt_t = dwt.rearrange("(ic p) o -> p ic o", p=128)

    sb = lambda name, shape, dt_: st.enter_context(nc.sbuf_tensor(name, shape, dt_))
    ps = lambda name, shape, dt_=F32: st.enter_context(nc.psum_tensor(name, shape, dt_))

    xn_sb = sb("xn_sb", [128, DC * TC], BF16)
    hdn = sb("hdn", [128, IC0 * TC], BF16)
    x1_sb = sb("x1_sb", [128, DC * TC], F32)
    xn1_sb = sb("xn1_sb", [128, DC * TC], BF16)
    gw_sb = sb("gw_sb", [128, DC * 128], BF16)
    uw_sb = sb("uw_sb", [128, DC * 128], BF16)
    dw_sb = sb("dw_sb", [128, IC0 * 128], BF16)
    gs_sb = sb("gs_sb", [128, IC0], F32)
    us_sb = sb("us_sb", [128, IC0], F32)
    ds_sb = sb("ds_sb", [128, DC], F32)
    xs_sb = sb("xs_sb", [128, TC], F32)
    ss_sb = sb("ss_sb", [128, TC], F32)
    gg = sb("gg", [128, TC], F32)
    uu = sb("uu", [128, TC], F32)
    th = sb("th", [128, TC], F32)
    t1c = sb("t1c", [128, TC], F32)
    ff = sb("ff", [128, TC], F32)
    g2 = sb("g2", [128, TC], F32)
    ones_b = sb("ones_b", [128, 1], BF16)
    ones_f = sb("ones_f", [1, 128], F32)
    cst = sb("cst", [128, 2], F32)
    sq = sb("sq", [128, DC * TC], BF16)
    xc = sb("xc", [128, DC * TC], BF16)
    rows = sb("rows", [1, 5 * TC], F32)
    t1 = sb("t1", [128, TC], F32)

    p_g = ps("p_g", [128, TC])
    p_u = ps("p_u", [128, TC])
    p_d = ps("p_d", [128, TC])
    p_mean = ps("p_mean", [1, TC])
    p_msq = ps("p_msq", [1, TC])
    p_a = ps("p_a", [128, TC])
    p_b = ps("p_b", [128, TC])

    ln_bufs = dict(sq=sq, xc=xc, rows=rows, p_mean=p_mean, p_msq=p_msq,
                   p_a=p_a, p_b=p_b, ones_b=ones_b, ones_f=ones_f, cst=cst, t1=t1)

    ev_gs = P.dma("gpsimd", gs_sb[:, :], gs_i[:, :])
    ev_us = P.dma("gpsimd", us_sb[:, :], us_i[:, :])
    ev_ds = P.dma("gpsimd", ds_sb[:, :], ds_i[:, :])
    ev_of = P.dma("gpsimd", ones_f[:, :], onesf[:, :])
    ev_ob = P.op("gpsimd", lambda e: e.memset(ones_b[:, :], 1.0))
    P.op("gpsimd", lambda e: e.memset(cst[:, 0:1], 1e-5))
    ev_cst = P.op("gpsimd", lambda e: e.memset(cst[:, 1:2], 1.0))
    ev_us2 = P.op("vector", lambda e: e.tensor_scalar(us_sb[:, :], us_sb[:, :], 0.5, None, ALU.mult), deps=[ev_us])

    ev_gw_used = ev_uw_used = ev_dw_used = None
    ev_xn_load = None
    ev_th_r = None
    ev_gg_r = None
    ev_ff_r = None
    prev_stats = prev_bcast = None
    ev_x1store = ev_xn1store = None
    ev_evg = ev_evu = None
    ev_hdn_used = None

    for tc in range(NTC):
        tsl = slice(tc * TC, (tc + 1) * TC)
        ev_xn_load = P.dma("sync", xn_sb[:, :].rearrange("p (c t) -> p c t", t=TC),
                           xn_t[:, :, tsl], deps=[P.last_ev["tensor"]])
        for ic in range(IC0):
            ev_gw = P.dma("gpsimd", gw_sb[:, :].rearrange("p (c i) -> p c i", i=128),
                          gwt_t[:, :, ic * 128:(ic + 1) * 128], deps=[ev_gw_used])
            ev_uw = P.dma("gpsimd", uw_sb[:, :].rearrange("p (c i) -> p c i", i=128),
                          uwt_t[:, :, ic * 128:(ic + 1) * 128], deps=[ev_uw_used])
            for c in range(DC):
                last = c == DC - 1
                P.op("tensor", lambda e, c=c, last=last: e.matmul(
                    p_g[:, :], gw_sb[:, c * 128:(c + 1) * 128], xn_sb[:, c * TC:(c + 1) * TC],
                    start=(c == 0), stop=last),
                    deps=[ev_gw, ev_xn_load, ev_evg] if c == 0 else (), inc=1 if last else 0)
            ev_mm_g = P.last_ev["tensor"]
            ev_gw_used = ev_mm_g
            for c in range(DC):
                last = c == DC - 1
                P.op("tensor", lambda e, c=c, last=last: e.matmul(
                    p_u[:, :], uw_sb[:, c * 128:(c + 1) * 128], xn_sb[:, c * TC:(c + 1) * TC],
                    start=(c == 0), stop=last),
                    deps=[ev_uw, ev_xn_load, ev_evu] if c == 0 else (), inc=1 if last else 0)
            ev_mm_u = P.last_ev["tensor"]
            ev_uw_used = ev_mm_u
            ev_evg = P.op("vector", lambda e, ic=ic: e.tensor_scalar(
                gg[:, :], p_g[:, :], gs_sb[:, ic:ic + 1], None, ALU.mult), deps=[ev_mm_g, ev_gs, ev_gg_r])
            ev_evu = P.op("vector", lambda e, ic=ic: e.tensor_scalar(
                uu[:, :], p_u[:, :], us_sb[:, ic:ic + 1], None, ALU.mult), deps=[ev_mm_u, ev_us2])
            ev_th = P.op("scalar", lambda e: e.activation(th[:, :], gg[:, :], AF.Tanh, scale=0.5), deps=[ev_evg, ev_th_r])
            ev_gg_r = ev_th
            ev_t1 = P.op("vector", lambda e: e.scalar_tensor_tensor(
                t1c[:, :], th[:, :], 1.0, gg[:, :], ALU.add, ALU.mult), deps=[ev_th])
            ev_th_r = ev_t1
            ev_hdn = P.op("vector", lambda e, ic=ic: e.tensor_mul(
                hdn[:, ic * TC:(ic + 1) * TC], t1c[:, :], uu[:, :]), deps=[ev_evu, ev_hdn_used])
        ev_x1_all = []
        for hc in range(DC):
            ev_dw = P.dma("gpsimd", dw_sb[:, :].rearrange("p (ic o) -> p ic o", o=128),
                          dwt_t[:, :, hc * 128:(hc + 1) * 128], deps=[ev_dw_used])
            ev_xs = P.dma("sync", xs_sb[:, :], x_t[:, hc, tsl], deps=[P.last_ev["vector"]])
            ev_ss = P.dma("sync", ss_sb[:, :], s_t[:, hc, tsl], deps=[P.last_ev["vector"]])
            for ic in range(IC0):
                last = ic == IC0 - 1
                P.op("tensor", lambda e, ic=ic, last=last: e.matmul(
                    p_d[:, :], dw_sb[:, ic * 128:(ic + 1) * 128], hdn[:, ic * TC:(ic + 1) * TC],
                    start=(ic == 0), stop=last),
                    deps=[ev_dw, P.last_ev["vector"]] if ic == 0 else (), inc=1 if last else 0)
            ev_mm_d = P.last_ev["tensor"]
            ev_dw_used = ev_mm_d
            if hc == DC - 1:
                ev_hdn_used = ev_mm_d
            ev_ff = P.op("vector", lambda e, hc=hc: e.tensor_scalar(
                ff[:, :], p_d[:, :], ds_sb[:, hc:hc + 1], None, ALU.mult), deps=[ev_mm_d, ev_ds, ev_ff_r])
            ev_th2 = P.op("scalar", lambda e: e.activation(th[:, :], ff[:, :], AF.Tanh, scale=0.5),
                          deps=[ev_ff, ev_th_r])
            ev_ff_r = ev_th2
            ev_g2 = P.op("vector", lambda e: e.scalar_tensor_tensor(
                g2[:, :], th[:, :], 1.0, ff[:, :], ALU.add, ALU.mult), deps=[ev_th2])
            ev_th_r = ev_g2
            ev_sg = P.op("vector", lambda e: e.tensor_mul(g2[:, :], g2[:, :], ss_sb[:, :]), deps=[ev_ss])
            ev_x1 = P.op("vector", lambda e, hc=hc: e.tensor_add(
                x1_sb[:, hc * TC:(hc + 1) * TC], xs_sb[:, :], g2[:, :]), deps=[ev_xs, ev_x1store])
            ev_x1_all.append(ev_x1)
        ev_x1store = P.dma("sync", x1_t[:, :, tsl], x1_sb[:, :].rearrange("p (c t) -> p c t", t=TC),
                           deps=[ev_x1_all[-1]])
        ev_ln, prev_stats, prev_bcast = emit_ln(
            P, ln_bufs, x1_sb[:, :], xn1_sb[:, :], DC, TC,
            deps=[ev_x1_all[-1], ev_ob, ev_of, ev_cst, ev_xn1store],
            prev_stats_done=prev_stats, prev_bcast_done=prev_bcast)
        ev_xn1store = P.dma("sync", xn1_t[:, :, tsl], xn1_sb[:, :].rearrange("p (c t) -> p c t", t=TC),
                            deps=[ev_ln])


def build_C():
    nc = bass.Bass("TRN2", target_bir_lowering=False)
    return build_and_finalize(nc, emit_C)


def prep_C_inputs(inputs, resA, resB):
    gwt = np.ascontiguousarray(inputs["L0_ffn_gate_w"].T)
    uwt = np.ascontiguousarray(inputs["L0_ffn_up_w"].T)
    dwt = np.ascontiguousarray(inputs["L0_ffn_down_w"].T)
    gs = np.ascontiguousarray(inputs["L0_ffn_gate_s"].reshape(IC0, 128).T)
    us = np.ascontiguousarray(inputs["L0_ffn_up_s"].reshape(IC0, 128).T)
    ds = np.ascontiguousarray(inputs["L0_ffn_down_s"].reshape(DC, 128).T)
    onesf = np.ones((1, 128), np.float32)
    x_full = inputs["L0_attn_x"]
    maps = []
    for c in range(NCORES):
        b, h = c // 2, c % 2
        tloc = slice(1024 * 0, T_LOC)
        tofs = h * T_LOC
        s_loc = np.concatenate([resB[2 * b]["s"][:, tofs:tofs + T_LOC],
                                resB[2 * b + 1]["s"][:, tofs:tofs + T_LOC]], axis=0)
        maps.append({"xn": resA[c]["xn"], "x": _fm(_core_tokens(x_full, c)),
                     "s": np.ascontiguousarray(s_loc),
                     "gwt": gwt, "uwt": uwt, "dwt": dwt,
                     "gs": gs, "us": us, "ds": ds, "onesf": onesf})
    return maps


def run_C(inputs, resA, resB):
    nc = build_C()
    maps = prep_C_inputs(inputs, resA, resB)
    res = run_bass_kernel_spmd(nc, maps, core_ids=list(range(NCORES)))
    return res.results




def emit_D(nc, P, st):
    xn_i = nc.dram_tensor("xn1f", [D, T_FULL], BF16, kind="ExternalInput").ap()
    wq_i = nc.dram_tensor("wq", [D, 1024], F32, kind="ExternalInput").ap()
    wk_i = nc.dram_tensor("wk", [D, 256], F32, kind="ExternalInput").ap()
    wv_i = nc.dram_tensor("wv", [D, 256], F32, kind="ExternalInput").ap()
    cosr_i = nc.dram_tensor("cosr", [128, T_FULL], F32, kind="ExternalInput").ap()
    sins_i = nc.dram_tensor("sins", [128, T_FULL], F32, kind="ExternalInput").ap()
    perm_i = nc.dram_tensor("perm", [128, 128], F32, kind="ExternalInput").ap()
    id2_i = nc.dram_tensor("id2", [128, 64], F32, kind="ExternalInput").ap()
    masks_i = nc.dram_tensor("masks", [128, 4 * 512], F32, kind="ExternalInput").ap()
    onesf_i = nc.dram_tensor("onesf", [1, 128], F32, kind="ExternalInput").ap()
    ctx_o = nc.dram_tensor("ctx", [1024, T_FULL], BF16, kind="ExternalOutput").ap()

    TC = 512
    NTC = T_FULL // TC
    NOC_Q = 8
    xn_t = xn_i.rearrange("(c p) t -> p c t", p=128)
    wq_t = wq_i.rearrange("(c p) o -> p c o", p=128)
    wk_t = wk_i.rearrange("(c p) o -> p c o", p=128)
    wv_t = wv_i.rearrange("(c p) o -> p c o", p=128)

    sb = lambda name, shape, dt_: st.enter_context(nc.sbuf_tensor(name, shape, dt_))
    ps = lambda name, shape, dt_=F32: st.enter_context(nc.psum_tensor(name, shape, dt_))

    xn_sb = sb("xn_sb", [128, DC * TC], BF16)
    w_sb = [sb(f"w_sb{i}", [128, DC * 128], BF16) for i in range(2)]
    q_sb = sb("q_sb", [128, NOC_Q * T_FULL], BF16)
    k_sb = sb("k_sb", [128, 4 * T_FULL], BF16)
    v_sb = sb("v_sb", [128, 4 * 32 * 65], BF16)
    v_tmp = sb("v_tmp", [128, TC], BF16)
    qf_bf = sb("qf_bf", [128, TC], BF16)
    krope = sb("krope", [128, TC], BF16)
    At = sb("At", [128, TC], F32)
    cos_sb = sb("cos_sb", [128, TC], F32)
    sin_sb = sb("sin_sb", [128, TC], F32)
    perm_sb = sb("perm_sb", [128, 128], BF16)
    id2_sb = sb("id2_sb", [128, 64], BF16)
    masks_sb = sb("masks_sb", [128, 4 * 512], BF16)
    ones_f = sb("ones_f", [1, 128], F32)
    ones_bk = sb("ones_bk", [128, 1], BF16)
    q2t = sb("q2t", [128, TC], BF16)
    qq = sb("qq", [1, 32], F32)
    brow = sb("brow", [1, 16], F32)
    biasT = sb("biasT", [128, 16], F32)
    pT = [sb(f"pT{i}", [128, TC], BF16) for i in range(2)]
    lrow = sb("lrow", [1, TC], F32)
    ctxf = sb("ctxf", [64, TC], F32)
    outbf = sb("outbf", [64, TC], BF16)

    p_mm = ps("p_mm", [128, TC])
    p_sw = ps("p_sw", [128, TC])
    p_T = ps("p_T", [128, 64], BF16)
    p_s = [ps(f"p_s{i}", [128, TC]) for i in range(2)]
    p_ctx = ps("p_ctx", [65, TC])
    p_lb = ps("p_lb", [64, TC])
    p_n = ps("p_n", [1, TC])

    ev_perm = P.dma("gpsimd", perm_sb[:, :], perm_i[:, :])
    ev_id2 = P.dma("gpsimd", id2_sb[:, :], id2_i[:, :])
    ev_msk = P.dma("gpsimd", masks_sb[:, :], masks_i[:, :])
    ev_of = P.dma("gpsimd", ones_f[:, :], onesf_i[:, :])
    ev_ob = P.op("gpsimd", lambda e: e.memset(ones_bk[:, :], 1.0))
    ev_vones = P.op("gpsimd", lambda e: e.memset(v_sb[:, :], 1.0))

    chunks = [("q", i) for i in range(NOC_Q)] + [("k", i) for i in range(2)] + [("v", i) for i in range(2)]
    ev_w_used = [None, None]
    ev_evac = None
    ev_rope_last = None
    ev_mmdone_prev = None
    ev_vt_done = None
    ev_cs_used = None
    for tc in range(NTC):
        tsl = slice(tc * TC, (tc + 1) * TC)
        ev_xn = P.dma("sync", xn_sb[:, :].rearrange("p (c t) -> p c t", t=TC),
                      xn_t[:, :, tsl], deps=[ev_mmdone_prev])
        ev_cos = P.dma("sync", cos_sb[:, :], cosr_i[:, tsl], deps=[ev_cs_used])
        ev_sin = P.dma("sync", sin_sb[:, :], sins_i[:, tsl], deps=[ev_cs_used])
        for ci, (kind, oc) in enumerate(chunks):
            wbuf = ci % 2
            w_t = {"q": wq_t, "k": wk_t, "v": wv_t}[kind]
            ev_w = P.dma("gpsimd", w_sb[wbuf][:, :].rearrange("p (c o) -> p c o", o=128),
                         w_t[:, :, oc * 128:(oc + 1) * 128], deps=[ev_w_used[wbuf]])
            for c in range(DC):
                last = c == DC - 1
                P.op("tensor", lambda e, c=c, last=last, wbuf=wbuf: e.matmul(
                    p_mm[:, :], w_sb[wbuf][:, c * 128:(c + 1) * 128], xn_sb[:, c * TC:(c + 1) * TC],
                    start=(c == 0), stop=last),
                    deps=[ev_w, ev_xn, ev_evac] if c == 0 else (), inc=1 if last else 0)
            ev_mm = P.last_ev["tensor"]
            ev_w_used[wbuf] = ev_mm
            if kind == "v":
                ev_evac = P.op("vector", lambda e: e.tensor_copy(v_tmp[:, :], p_mm[:, :]),
                               deps=[ev_mm, ev_vt_done])
                for half in range(2):
                    g = 2 * oc + half
                    hb = 64 * half
                    for j in range(4):
                        ev_tp = P.op("tensor", lambda e, hb=hb, j=j: e.transpose(
                            p_T[:, :], v_tmp[hb:hb + 64, j * 128:(j + 1) * 128], id2_sb[hb:hb + 64, :]),
                            deps=[ev_evac, ev_id2, ev_vt_done], inc=1)
                        ev_vt_done = P.op("vector", lambda e, g=g, j=j, tc=tc: e.tensor_copy(
                            v_sb[:, (g * 32 + tc * 4 + j) * 65:(g * 32 + tc * 4 + j) * 65 + 64],
                            p_T[:, :]), deps=[ev_tp, ev_vones])
            else:
                ev_evac = P.op("vector", lambda e: e.tensor_copy(qf_bf[:, :], p_mm[:, :]),
                               deps=[ev_mm, ev_rope_last])
                ev_sw = P.op("tensor", lambda e: e.matmul(
                    p_sw[:, :], perm_sb[:, :], qf_bf[:, :], start=True, stop=True),
                    deps=[ev_evac, ev_perm, ev_rope_last], inc=1)
                ev_At = P.op("vector", lambda e: e.tensor_mul(At[:, :], qf_bf[:, :], cos_sb[:, :]),
                             deps=[ev_evac, ev_cos])
                if kind == "q":
                    dst = q_sb[:, oc * T_FULL + tc * TC: oc * T_FULL + (tc + 1) * TC]
                    ev_r = P.op("vector", lambda e, dst=dst: e.tensor_mul(krope[:, :], p_sw[:, :], sin_sb[:, :]),
                                deps=[ev_sw, ev_sin])
                    ev_rope_last = P.op("vector", lambda e, dst=dst: e.tensor_add(dst, At[:, :], krope[:, :]))
                else:
                    ev_r = P.op("vector", lambda e: e.tensor_mul(krope[:, :], p_sw[:, :], sin_sb[:, :]),
                                deps=[ev_sw, ev_sin])
                    ev_rope_last = P.op("vector", lambda e: e.tensor_add(krope[:, :], At[:, :], krope[:, :]))
                    for half in range(2):
                        g = 2 * oc + half
                        hb = 64 * half
                        col = g * T_FULL + tc * TC
                        P.dma("gpsimd", k_sb[0:64, col:col + TC], krope[hb:hb + 64, :], deps=[ev_rope_last])
                        ev_kdup = P.dma("gpsimd", k_sb[64:128, col:col + TC], krope[hb:hb + 64, :])
            ev_cs_used = ev_rope_last
        ev_mmdone_prev = P.last_ev["tensor"]
    ev_kdup_done = P.last_ev["gpsimd"]

    ev_sq_prev = None
    ev_qq = []
    for idx in range(20):
        if idx < 16:
            qh = idx
            base, oc = 64 * (qh % 2), qh // 2
            src = lambda t_, base=base, oc=oc: q_sb[base:base + 64, oc * T_FULL + t_ * TC: oc * T_FULL + (t_ + 1) * TC]
            dep_data = ev_rope_last
        else:
            g = idx - 16
            base = 0
            src = lambda t_, g=g: k_sb[0:64, g * T_FULL + t_ * TC: g * T_FULL + (t_ + 1) * TC]
            dep_data = ev_kdup_done
        for t_ in range(NTC):
            ev_q2 = P.op("scalar", lambda e, t_=t_, base=base, src=src: e.activation(
                q2t[base:base + 64, :], src(t_), AF.Square), deps=[dep_data, ev_sq_prev])
            ev_nmm = P.op("tensor", lambda e, base=base: e.matmul(
                p_n[:, :], ones_bk[base:base + 64, :], q2t[base:base + 64, :], start=True, stop=True),
                deps=[ev_q2, ev_ob, ev_sq_prev], inc=1)
            ev_red = P.op("vector", lambda e, idx=idx: e.reduce_max(
                qq[0:1, 31:32], p_n[0:1, :], axis=AX.X), deps=[ev_nmm])
            if t_ == 0:
                ev_mx = P.op("vector", lambda e, idx=idx: e.tensor_copy(qq[0:1, idx:idx + 1], qq[0:1, 31:32]))
            else:
                ev_mx = P.op("vector", lambda e, idx=idx: e.tensor_max(
                    qq[0:1, idx:idx + 1], qq[0:1, idx:idx + 1], qq[0:1, 31:32]))
            ev_sq_prev = ev_mx
    ev_bh = []
    for qh in range(16):
        g = qh // 4
        P.op("vector", lambda e, qh=qh, g=g: e.tensor_mul(
            brow[0:1, qh:qh + 1], qq[0:1, qh:qh + 1], qq[0:1, 16 + g:17 + g]), deps=[ev_sq_prev])
    ev_ln = P.op("scalar", lambda e: e.activation(brow[0:1, :], brow[0:1, :], AF.Ln),
                 deps=[P.last_ev["vector"]])
    ev_sqr = P.op("scalar", lambda e: e.activation(brow[0:1, :], brow[0:1, :], AF.Exp, scale=0.5), deps=[ev_ln])
    ev_bb = P.op("tensor", lambda e: e.matmul(p_lb[:, 0:16].rearrange("p x -> p x"), ones_f[0:1, 0:64],
                                              brow[0:1, :], start=True, stop=True),
                 deps=[ev_sqr, ev_of], inc=1)
    ev_biasT = P.op("vector", lambda e: e.tensor_scalar(biasT[0:64, :], p_lb[:, 0:16], -0.125, None, ALU.mult),
                    deps=[ev_bb])
    ev_biasT2 = P.dma("gpsimd", biasT[64:128, :], biasT[0:64, :], deps=[ev_biasT])

    ev_exp_done = [None, None]
    ev_pv_done = [None, None]
    ev_ctx_free = None
    ev_lb_free = ev_biasT2
    ev_ctxstore = None
    for qh in range(16):
        g = qh // 4
        base, oc = 64 * (qh % 2), qh // 2
        for qc in range(NTC):
            n_kt = 4 * qc + 4
            ev_exp_evs = {}
            for kt in range(n_kt):
                buf = kt % 2
                ev_s = P.op("tensor", lambda e, buf=buf, kt=kt, g=g, base=base, oc=oc, qc=qc: e.matmul(
                    p_s[buf][:, :],
                    k_sb[base:base + 64, g * T_FULL + kt * 128: g * T_FULL + (kt + 1) * 128],
                    q_sb[base:base + 64, oc * T_FULL + qc * TC: oc * T_FULL + (qc + 1) * TC],
                    start=True, stop=True),
                    deps=[ev_exp_done[buf], ev_rope_last, ev_kdup_done], inc=1)
                ev_e = P.op("scalar", lambda e, buf=buf, qh=qh: e.activation(
                    pT[buf][:, :], p_s[buf][:, :], AF.Exp, scale=0.125, bias=biasT[:, qh:qh + 1]),
                    deps=[ev_s, ev_pv_done[buf], ev_biasT2])
                ev_exp_done[buf] = ev_e
                ev_pin = ev_e
                if kt >= 4 * qc:
                    j = kt - 4 * qc
                    ev_pin = P.op("vector", lambda e, buf=buf, j=j: e.tensor_mul(
                        pT[buf][:, :], pT[buf][:, :], masks_sb[:, j * 512:(j + 1) * 512]),
                        deps=[ev_e, ev_msk])
                ev_pv = P.op("tensor", lambda e, buf=buf, kt=kt, g=g, n_kt=n_kt: e.matmul(
                    p_ctx[:, :], v_sb[:, (g * 32 + kt) * 65:(g * 32 + kt) * 65 + 65], pT[buf][:, :],
                    start=(kt == 0), stop=(kt == n_kt - 1)),
                    deps=[ev_pin, ev_vt_done] + ([ev_ctx_free] if kt == 0 else []), inc=1)
                ev_pv_done[buf] = ev_pv
            ev_li = P.op("vector", lambda e: e.reciprocal(lrow[0:1, :], p_ctx[64:65, :]), deps=[ev_pv])
            ev_lb = P.op("tensor", lambda e: e.matmul(
                p_lb[:, :], ones_f[0:1, 0:64], lrow[0:1, :], start=True, stop=True),
                deps=[ev_li, ev_lb_free], inc=1)
            ev_cf = P.op("vector", lambda e: e.tensor_copy(ctxf[:, :], p_ctx[0:64, :]), deps=[ev_pv])
            ev_ctx_free = ev_cf
            ev_ob2 = P.op("vector", lambda e: e.tensor_mul(outbf[:, :], ctxf[:, :], p_lb[:, :]),
                          deps=[ev_lb, ev_cf, ev_ctxstore])
            ev_lb_free = ev_ob2
            ev_ctxstore = P.dma("sync", ctx_o[qh * 64:(qh + 1) * 64, qc * TC:(qc + 1) * TC],
                                outbf[:, :], deps=[ev_ob2])


def build_D():
    nc = bass.Bass("TRN2", target_bir_lowering=False)
    return build_and_finalize(nc, emit_D)


def _rope_consts():
    hd = 64
    inv_freq = 1.0 / (10000.0 ** (np.arange(0, hd, 2, dtype=np.float64) / hd))
    t = np.arange(T_FULL, dtype=np.float64)
    fr = np.outer(t, inv_freq)
    cosT = np.cos(fr).T.astype(np.float32)
    sinT = np.sin(fr).T.astype(np.float32)
    cosr = np.tile(cosT, (4, 1))
    sins = np.concatenate([-sinT, sinT] * 2, axis=0)
    perm = np.zeros((128, 128), np.float32)
    for m_ in range(128):
        b_ = 64 * (m_ // 64)
        src = b_ + (m_ - b_ + 32) % 64
        perm[src, m_] = 1.0
    id2 = np.concatenate([np.eye(64, dtype=np.float32)] * 2, axis=0)
    masks = np.zeros((128, 4 * 512), np.float32)
    for j in range(4):
        for p_ in range(128):
            ts0 = j * 128 + p_
            if ts0 < 512:
                masks[p_, j * 512 + ts0:(j + 1) * 512] = 1.0
    return cosr, sins, perm, id2, masks


def prep_D_inputs(inputs, resC):
    cosr, sins, perm, id2, masks = _rope_consts()
    onesf = np.ones((1, 128), np.float32)
    Wq, Wk, Wv = inputs["L1_attn_Wq"], inputs["L1_attn_Wk"], inputs["L1_attn_Wv"]
    maps = []
    for c in range(NCORES):
        b, h = c // 2, c % 2
        xn1f = np.concatenate([resC[2 * b]["xn1"], resC[2 * b + 1]["xn1"]], axis=1)
        maps.append({"xn1f": np.ascontiguousarray(xn1f),
                     "wq": np.ascontiguousarray(Wq[:, 1024 * h:1024 * h + 1024]),
                     "wk": np.ascontiguousarray(Wk[:, 256 * h:256 * h + 256]),
                     "wv": np.ascontiguousarray(Wv[:, 256 * h:256 * h + 256]),
                     "cosr": cosr, "sins": sins, "perm": perm, "id2": id2,
                     "masks": masks, "onesf": onesf})
    return maps


def run_D(inputs, resC):
    nc = build_D()
    maps = prep_D_inputs(inputs, resC)
    res = run_bass_kernel_spmd(nc, maps, core_ids=list(range(NCORES)))
    return res.results




IC1 = 5504 // 128


def emit_E0(nc, P, st):
    gi = nc.dram_tensor("gi", [D, 688], I32, kind="ExternalInput").ap()
    ui = nc.dram_tensor("ui", [D, 688], I32, kind="ExternalInput").ap()
    di = nc.dram_tensor("di", [5504, 256], I32, kind="ExternalInput").ap()
    lut_i = nc.dram_tensor("lut", [1, 16], F32, kind="ExternalInput").ap()
    wg_o = nc.dram_tensor("wg", [D, 688], BF16, kind="ExternalOutput").ap()
    wu_o = nc.dram_tensor("wu", [D, 688], BF16, kind="ExternalOutput").ap()
    wd_o = nc.dram_tensor("wd", [5504, 256], BF16, kind="ExternalOutput").ap()

    sb = lambda name, shape, dt_: st.enter_context(nc.sbuf_tensor(name, shape, dt_))
    idx_sb = sb("idx_sb", [128, 688], I32)
    ibf = sb("ibf", [128, 688], BF16)
    tmp = sb("tmp", [128, 688], BF16)
    acc = sb("acc", [128, 688], BF16)
    lutb = sb("lutb", [128, 16], F32)

    ev_lut = P.dma("gpsimd", lutb[:, :], lut_i[:, :].partition_broadcast(128))
    ev_store = None
    for (src_t, dst_t, nch, w_) in ((gi, wg_o, DC, 688), (ui, wu_o, DC, 688), (di, wd_o, IC1, 256)):
        src_r = src_t.rearrange("(c p) w -> p c w", p=128)
        dst_r = dst_t.rearrange("(c p) w -> p c w", p=128)
        for ch in range(nch):
            ev_ld = P.dma("sync", idx_sb[:, 0:w_], src_r[:, ch, :], deps=[P.last_ev["vector"]])
            ev_cv = P.op("vector", lambda e, w_=w_: e.tensor_copy(ibf[:, 0:w_], idx_sb[:, 0:w_]), deps=[ev_ld])
            for k_ in range(16):
                if k_ == 0:
                    P.op("vector", lambda e, w_=w_: e.tensor_scalar(
                        acc[:, 0:w_], ibf[:, 0:w_], 0.0, lutb[:, 0:1], ALU.is_equal, ALU.mult),
                        deps=[ev_lut, ev_store])
                else:
                    P.op("vector", lambda e, k_=k_, w_=w_: e.tensor_scalar(
                        tmp[:, 0:w_], ibf[:, 0:w_], float(k_), lutb[:, k_:k_ + 1], ALU.is_equal, ALU.mult))
                    P.op("vector", lambda e, w_=w_: e.tensor_add(acc[:, 0:w_], acc[:, 0:w_], tmp[:, 0:w_]))
            ev_store = P.dma("sync", dst_r[:, ch, :], acc[:, 0:w_], deps=[P.last_ev["vector"]])


def build_E0():
    nc = bass.Bass("TRN2", target_bir_lowering=False)
    return build_and_finalize(nc, emit_E0)


def run_E0(inputs):
    nc = build_E0()
    giT = np.ascontiguousarray(inputs["L1_ffn_gate_idx"].T)
    uiT = np.ascontiguousarray(inputs["L1_ffn_up_idx"].T)
    diT = np.ascontiguousarray(inputs["L1_ffn_down_idx"].T)
    lut = np.ascontiguousarray(inputs["L1_ffn_lut"].reshape(1, 16))
    maps = []
    for c in range(NCORES):
        maps.append({"gi": np.ascontiguousarray(giT[:, c * 688:(c + 1) * 688]),
                     "ui": np.ascontiguousarray(uiT[:, c * 688:(c + 1) * 688]),
                     "di": np.ascontiguousarray(diT[:, c * 256:(c + 1) * 256]),
                     "lut": lut})
    res = run_bass_kernel_spmd(nc, maps, core_ids=list(range(NCORES)))
    return res.results


def emit_E(nc, P, st):
    ctx_i = nc.dram_tensor("ctxf", [D, T_LOC], BF16, kind="ExternalInput").ap()
    x1_i = nc.dram_tensor("x1", [D, T_LOC], F32, kind="ExternalInput").ap()
    xn1_i = nc.dram_tensor("xn1", [D, T_LOC], BF16, kind="ExternalInput").ap()
    wo_i = nc.dram_tensor("wo", [D, D], F32, kind="ExternalInput").ap()
    bo_i = nc.dram_tensor("bo", [128, DC], F32, kind="ExternalInput").ap()
    wg_i = nc.dram_tensor("wg", [D, 5504], BF16, kind="ExternalInput").ap()
    wu_i = nc.dram_tensor("wu", [D, 5504], BF16, kind="ExternalInput").ap()
    wd_i = nc.dram_tensor("wd", [5504, D], BF16, kind="ExternalInput").ap()
    out_o = nc.dram_tensor("out", [D, T_LOC], F32, kind="ExternalOutput").ap()

    TC = 512
    NTC = T_LOC // TC
    ctx_t = ctx_i.rearrange("(c p) t -> p c t", p=128)
    x1_t = x1_i.rearrange("(c p) t -> p c t", p=128)
    xn1_t = xn1_i.rearrange("(c p) t -> p c t", p=128)
    out_t = out_o.rearrange("(c p) t -> p c t", p=128)
    wo_t = wo_i.rearrange("(c p) o -> p c o", p=128)
    wg_t = wg_i.rearrange("(c p) i -> p c i", p=128)
    wu_t = wu_i.rearrange("(c p) i -> p c i", p=128)
    wd_t = wd_i.rearrange("(ic p) o -> p ic o", p=128)

    sb = lambda name, shape, dt_: st.enter_context(nc.sbuf_tensor(name, shape, dt_))
    ps = lambda name, shape, dt_=F32: st.enter_context(nc.psum_tensor(name, shape, dt_))

    xn_sb = sb("xn_sb", [128, DC * TC], BF16)
    ctx_sb = sb("ctx_sb", [128, DC * TC], BF16)
    s1_sb = sb("s1_sb", [128, DC * TC], F32)
    hdn = sb("hdn", [128, IC1 * TC], BF16)
    wo_sb = [sb(f"wo_sb{i}", [128, DC * 128], BF16) for i in range(2)]
    wgu_sb = [sb(f"wgu_sb{i}", [128, DC * 128], BF16) for i in range(4)]
    wd_sb = [sb(f"wd_sb{i}", [128, IC1 * 128], BF16) for i in range(2)]
    bo_sb = sb("bo_sb", [128, DC], F32)
    xs_sb = sb("xs_sb", [128, TC], F32)
    gg = sb("gg", [128, TC], F32)
    uu = sb("uu", [128, TC], F32)
    th = sb("th", [128, TC], F32)
    t1c = sb("t1c", [128, TC], F32)
    t2c = sb("t2c", [128, TC], F32)
    ff = sb("ff", [128, TC], F32)
    g2 = sb("g2", [128, TC], F32)
    out_sb = sb("out_sb", [128, TC], F32)

    p_o = ps("p_o", [128, TC])
    p_g = ps("p_g", [128, TC])
    p_u = ps("p_u", [128, TC])
    p_d = ps("p_d", [128, TC])

    ev_bo = P.dma("gpsimd", bo_sb[:, :], bo_i[:, :])

    ev_w_used = {}
    ev_s1_done = None
    ev_s1_war = None
    ev_hdn_used = None
    ev_outstore = None
    ev_th_r = None
    ev_gg_r = None
    ev_ff_r = None
    ev_evo = ev_evg = ev_evu = None
    for tc in range(NTC):
        tsl = slice(tc * TC, (tc + 1) * TC)
        ev_xn = P.dma("sync", xn_sb[:, :].rearrange("p (c t) -> p c t", t=TC),
                      xn1_t[:, :, tsl], deps=[P.last_ev["tensor"]])
        ev_ctx = P.dma("sync", ctx_sb[:, :].rearrange("p (c t) -> p c t", t=TC),
                       ctx_t[:, :, tsl], deps=[P.last_ev["tensor"]])
        for hc in range(DC):
            wb = hc % 2
            ev_w = P.dma("gpsimd", wo_sb[wb][:, :].rearrange("p (c o) -> p c o", o=128),
                         wo_t[:, :, hc * 128:(hc + 1) * 128], deps=[ev_w_used.get(("o", wb))])
            for c in range(DC):
                last = c == DC - 1
                P.op("tensor", lambda e, c=c, last=last, wb=wb: e.matmul(
                    p_o[:, :], wo_sb[wb][:, c * 128:(c + 1) * 128], ctx_sb[:, c * TC:(c + 1) * TC],
                    start=(c == 0), stop=last),
                    deps=[ev_w, ev_ctx, ev_evo] if c == 0 else (), inc=1 if last else 0)
            ev_mm = P.last_ev["tensor"]
            ev_w_used[("o", wb)] = ev_mm
            ev_evo = P.op("scalar", lambda e, hc=hc: e.activation(
                s1_sb[:, hc * TC:(hc + 1) * TC], p_o[:, :], AF.Identity, bias=bo_sb[:, hc:hc + 1]),
                deps=[ev_mm, ev_bo, ev_evo, ev_s1_war])
        ev_s1_done = ev_evo
        for ic in range(IC1):
            wbg, wbu = ic % 2, 2 + ic % 2
            ev_wg = P.dma("gpsimd", wgu_sb[wbg][:, :].rearrange("p (c i) -> p c i", i=128),
                          wg_t[:, :, ic * 128:(ic + 1) * 128], deps=[ev_w_used.get(("g", wbg))])
            ev_wu = P.dma("gpsimd", wgu_sb[wbu][:, :].rearrange("p (c i) -> p c i", i=128),
                          wu_t[:, :, ic * 128:(ic + 1) * 128], deps=[ev_w_used.get(("u", wbu))])
            for c in range(DC):
                last = c == DC - 1
                P.op("tensor", lambda e, c=c, last=last, wbg=wbg: e.matmul(
                    p_g[:, :], wgu_sb[wbg][:, c * 128:(c + 1) * 128], xn_sb[:, c * TC:(c + 1) * TC],
                    start=(c == 0), stop=last),
                    deps=[ev_wg, ev_xn, ev_evg] if c == 0 else (), inc=1 if last else 0)
            ev_mm_g = P.last_ev["tensor"]
            ev_w_used[("g", wbg)] = ev_mm_g
            for c in range(DC):
                last = c == DC - 1
                P.op("tensor", lambda e, c=c, last=last, wbu=wbu: e.matmul(
                    p_u[:, :], wgu_sb[wbu][:, c * 128:(c + 1) * 128], xn_sb[:, c * TC:(c + 1) * TC],
                    start=(c == 0), stop=last),
                    deps=[ev_wu, ev_xn, ev_evu] if c == 0 else (), inc=1 if last else 0)
            ev_mm_u = P.last_ev["tensor"]
            ev_w_used[("u", wbu)] = ev_mm_u
            ev_evg = P.op("vector", lambda e: e.tensor_copy(gg[:, :], p_g[:, :]), deps=[ev_mm_g, ev_gg_r])
            ev_evu = P.op("vector", lambda e: e.tensor_copy(uu[:, :], p_u[:, :]), deps=[ev_mm_u])
            ev_th = P.op("scalar", lambda e: e.activation(th[:, :], gg[:, :], AF.Tanh, scale=0.5),
                         deps=[ev_evg, ev_th_r])
            ev_gg_r = ev_th
            ev_t1 = P.op("vector", lambda e: e.scalar_tensor_tensor(
                t1c[:, :], th[:, :], 1.0, gg[:, :], ALU.add, ALU.mult), deps=[ev_th])
            ev_th_r = ev_t1
            ev_hdn = P.op("vector", lambda e, ic=ic: e.tensor_mul(
                hdn[:, ic * TC:(ic + 1) * TC], t1c[:, :], uu[:, :]), deps=[ev_hdn_used])
        for hc in range(DC):
            wb = hc % 2
            ev_dw = P.dma("gpsimd", wd_sb[wb][:, :].rearrange("p (ic o) -> p ic o", o=128),
                          wd_t[:, :, hc * 128:(hc + 1) * 128], deps=[ev_w_used.get(("d", wb))])
            ev_xs = P.dma("sync", xs_sb[:, :], x1_t[:, hc, tsl], deps=[P.last_ev["vector"]])
            for ic in range(IC1):
                last = ic == IC1 - 1
                P.op("tensor", lambda e, ic=ic, last=last, wb=wb: e.matmul(
                    p_d[:, :], wd_sb[wb][:, ic * 128:(ic + 1) * 128], hdn[:, ic * TC:(ic + 1) * TC],
                    start=(ic == 0), stop=last),
                    deps=[ev_dw, P.last_ev["vector"]] if ic == 0 else (), inc=1 if last else 0)
            ev_mm_d = P.last_ev["tensor"]
            ev_w_used[("d", wb)] = ev_mm_d
            if hc == DC - 1:
                ev_hdn_used = ev_mm_d
            ev_ff = P.op("vector", lambda e: e.tensor_scalar(ff[:, :], p_d[:, :], 0.5, None, ALU.mult),
                         deps=[ev_mm_d, ev_ff_r])
            ev_th2 = P.op("scalar", lambda e: e.activation(th[:, :], ff[:, :], AF.Tanh, scale=0.5),
                          deps=[ev_ff, ev_th_r])
            ev_ff_r = ev_th2
            ev_g2 = P.op("vector", lambda e: e.scalar_tensor_tensor(
                g2[:, :], th[:, :], 1.0, ff[:, :], ALU.add, ALU.mult), deps=[ev_th2])
            ev_th_r = ev_g2
            ev_sg = P.op("vector", lambda e, hc=hc: e.tensor_mul(
                g2[:, :], g2[:, :], s1_sb[:, hc * TC:(hc + 1) * TC]), deps=[ev_s1_done])
            ev_out = P.op("vector", lambda e: e.scalar_tensor_tensor(
                out_sb[:, :], g2[:, :], 0.5, xs_sb[:, :], ALU.mult, ALU.add),
                deps=[ev_xs, ev_outstore])
            ev_outstore = P.dma("sync", out_t[:, hc, tsl], out_sb[:, :], deps=[ev_out])
            ev_s1_war = ev_sg


def build_E():
    nc = bass.Bass("TRN2", target_bir_lowering=False)
    return build_and_finalize(nc, emit_E)


def prep_E_inputs(inputs, resC, resD, resE0):
    wg_full = np.concatenate([resE0[c]["wg"] for c in range(NCORES)], axis=1)
    wu_full = np.concatenate([resE0[c]["wu"] for c in range(NCORES)], axis=1)
    wd_full = np.concatenate([resE0[c]["wd"] for c in range(NCORES)], axis=1)
    bo = np.ascontiguousarray(inputs["L1_attn_b_out"].reshape(DC, 128).T)
    wo = inputs["L1_attn_Wo"]
    maps = []
    for c in range(NCORES):
        b, h = c // 2, c % 2
        tofs = h * T_LOC
        ctxf = np.concatenate([resD[2 * b]["ctx"][:, tofs:tofs + T_LOC],
                               resD[2 * b + 1]["ctx"][:, tofs:tofs + T_LOC]], axis=0)
        maps.append({"ctxf": np.ascontiguousarray(ctxf),
                     "x1": resC[c]["x1"], "xn1": resC[c]["xn1"],
                     "wo": wo, "bo": bo,
                     "wg": np.ascontiguousarray(wg_full), "wu": np.ascontiguousarray(wu_full),
                     "wd": np.ascontiguousarray(wd_full)})
    return maps


def run_E(inputs, resC, resD, resE0):
    nc = build_E()
    maps = prep_E_inputs(inputs, resC, resD, resE0)
    res = run_bass_kernel_spmd(nc, maps, core_ids=list(range(NCORES)))
    return res.results


def kernel(**inputs):
    x_full = inputs["L0_attn_x"]
    resA = run_A(inputs)
    resB = run_B(inputs, resA)
    resC = run_C(inputs, resA, resB)
    resD = run_D(inputs, resC)
    resE0 = run_E0(inputs)
    resE = run_E(inputs, resC, resD, resE0)
    out = np.empty((4, T_FULL, D), np.float32)
    for c in range(NCORES):
        b, h = c // 2, c % 2
        out[b, h * T_LOC:(h + 1) * T_LOC, :] = np.asarray(resE[c]["out"]).T
    return out
